# version 01 (final)
# baseline (speedup 1.0000x reference)
import numpy as np
import ml_dtypes
from contextlib import ExitStack

import concourse.bass as bass
import concourse.mybir as mybir
from concourse.bass_utils import run_bass_kernel_spmd

F32 = mybir.dt.float32
F16 = mybir.dt.float16
BF16 = mybir.dt.bfloat16
OP = mybir.AluOpType
AF = mybir.ActivationFunctionType
AX = mybir.AxisListType

D = 4096
DC = 32
SEQ = 2048
TOK = 1024
NH = 32
NKV = 8
HFF = 11008
HPAD = 11264
HSH = 1408
HT = 11
NC8 = 8

NF4 = np.array([
    -1.0, -0.6961928009986877, -0.5250730514526367, -0.39491748809814453,
    -0.28444138169288635, -0.18477343022823334, -0.09105003625154495, 0.0,
    0.07958029955625534, 0.16093020141124725, 0.24611230194568634,
    0.33791524171829224, 0.44070982933044434, 0.5626170039176941,
    0.7229568362236023, 1.0], dtype=np.float64)
BOUNDS = (NF4[:-1] + NF4[1:]) * 0.5
DELTAS = NF4[1:] - NF4[:-1]

HC_LIST = [(r, ht) for r in range(NC8) for ht in range(HT)
           if not (r == 7 and ht >= 9)]
assert len(HC_LIST) == 86

ROPE_INVF = (10000.0 ** (-np.arange(0, 128, 2, dtype=np.float32) / 128.0))
DQ_MODE = "serp4"


class Ctr:
    def __init__(self, handle):
        self.h = handle
        self.v = 0

    def inc(self, instr, n=1):
        instr.then_inc(self.h, n)
        self.v += n
        return self.v


class Gate:

    def __init__(self, ctr=None, val=0):
        self.ctr = ctr
        self.val = val

    def wait(self, eng):
        if self.ctr is not None and self.val > 0:
            eng.wait_ge(self.ctr.h, self.val)

    def set(self, ctr, val):
        self.ctr = ctr
        self.val = val

    def snap(self):
        return Gate(self.ctr, self.val)


def build_program(debug=()):
    nc = bass.Bass("TRN2", target_bir_lowering=False, debug=False,
                   num_devices=NC8)

    x_own = nc.dram_tensor("x_own", [TOK, D], F32, kind="ExternalInput").ap()
    x_ctx = nc.dram_tensor("x_ctx", [SEQ, D], F32, kind="ExternalInput").ap()
    wqkv_t = nc.dram_tensor("wqkv_t", [48, 128, DC, 128], F32,
                            kind="ExternalInput").ap()
    w_ob = nc.dram_tensor("w_ob", [8, 128, DC, 512], F32,
                          kind="ExternalInput").ap()
    w1s = nc.dram_tensor("w1s", [HSH, D], F32, kind="ExternalInput").ap()
    w2s = nc.dram_tensor("w2s", [HSH, D], F32, kind="ExternalInput").ap()
    w3s = nc.dram_tensor("w3s", [D, HSH], F32, kind="ExternalInput").ap()
    cosq = nc.dram_tensor("cosq", [128, TOK], BF16, kind="ExternalInput").ap()
    sinq = nc.dram_tensor("sinq", [128, TOK], BF16, kind="ExternalInput").ap()
    cosk = nc.dram_tensor("cosk", [128, SEQ], BF16, kind="ExternalInput").ap()
    sink = nc.dram_tensor("sink", [128, SEQ], BF16, kind="ExternalInput").ap()
    m01i = nc.dram_tensor("m01", [128, 8, SEQ], BF16,
                          kind="ExternalInput").ap()
    ident = nc.dram_tensor("ident", [128, 128], BF16,
                           kind="ExternalInput").ap()
    outt = nc.dram_tensor("out", [TOK, D], F32, kind="ExternalOutput").ap()

    xn_own_hbm = nc.dram_tensor("xn_own_hbm", [128, DC, TOK], BF16).ap()
    q_hbm = nc.dram_tensor("q_hbm", [NH, 128, TOK], BF16).ap()
    k_hbm = nc.dram_tensor("k_hbm", [NKV, 128, SEQ], BF16).ap()
    attT_hbm = nc.dram_tensor("attT_hbm", [128, DC, TOK], BF16).ap()
    s_hbm = nc.dram_tensor("s_hbm", [8, 128, D], BF16).ap()
    sh = {"w1": nc.dram_tensor("sh1", [HSH, D], BF16).ap(),
          "w2": nc.dram_tensor("sh2", [HSH, D], BF16).ap(),
          "w3": nc.dram_tensor("sh3", [HSH, D], BF16).ap()}
    ff = {"w1": nc.dram_tensor("f1", [NC8 * HSH, D], BF16).ap(),
          "w2": nc.dram_tensor("f2", [NC8 * HSH, D], BF16).ap(),
          "w3": nc.dram_tensor("f3", [NC8 * HSH, D], BF16).ap()}
    xnA_hbm = nc.dram_tensor("xnA_hbm", [128, DC, TOK], BF16).ap()
    xnB_hbm = nc.dram_tensor("xnB_hbm", [128, DC, TOK], BF16).ap()
    v_hbm = nc.dram_tensor("v_hbm", [128, 16, TOK], BF16).ap()

    d_at2 = None
    if "d_at2" in debug:
        d_at2 = nc.dram_tensor("d_at2", [128, DC, TOK], BF16,
                               kind="ExternalOutput").ap()
    dbg = {}
    for nm, srct in (("d_xn", xn_own_hbm), ("d_q", q_hbm), ("d_k", k_hbm),
                     ("d_xnA", xnA_hbm), ("d_xnB", xnB_hbm), ("d_v", v_hbm),
                     ("d_att", attT_hbm), ("d_s", s_hbm),
                     ("d_f1", ff["w1"]), ("d_f2", ff["w2"]),
                     ("d_f3", ff["w3"])):
        if nm in debug:
            dbg[nm] = (nc.dram_tensor(nm, list(srct.shape), BF16,
                                      kind="ExternalOutput").ap(), srct)

    es = ExitStack()

    def at(name, shape, dtype, off_bytes):
        return nc.alloc_sbuf_tensor_at(name, shape, dtype,
                                       offset=20 * 1024 + off_bytes).ap()

    KB = 1024
    idsb = at("idsb", [128, 128], BF16, 0)
    cqs = at("cqs", [128, 2, TOK], BF16, 1 * KB)
    cks = at("cks", [128, 2, SEQ], BF16, 5 * KB)
    sst = at("sst", [128, 16], F32, 13 * KB)
    lnst = at("lnst", [128, 16], F32, 13 * KB + 64)
    avsb = at("avsb", [128, 128], BF16, 13 * KB + 128)
    dqam = at("dqam", [128, 32], F32, 13 * KB + 384)
    cst = at("cst", [128, 4], F32, 13 * KB + 512)
    dqrc = at("dqrc", [128, 32], F32, 13 * KB + 640)
    A0 = 14 * KB

    def aA(name, shape, dtype, off_e2):
        return at(name, shape, dtype, A0 + 2 * off_e2)

    xnt = aA("xnt", [128, DC, 1024], BF16, 0)
    m01 = aA("m01s", [128, 8, SEQ], BF16, 0)
    psb = aA("psb", [128, 2, SEQ], BF16, 16384)
    ptsb = aA("ptsb", [128, 2, SEQ], BF16, 20480)
    qh = aA("qh", [128, 2, TOK], BF16, 24576)
    khh = aA("khh", [128, 2, SEQ], BF16, 26624)
    at_h = aA("at_h", [128, TOK], BF16, 30720)
    attT = aA("attT", [128, DC, 1024], BF16, 0)
    vsb = aA("vsb", [128, 16, 1024], BF16, 32768)
    wosb = aA("wosb", [128, DC, 512], BF16, 32768)
    wti = aA("wti", [128, 2, DC, 128], BF16, 49152)
    rxq = aA("rxq", [128, SEQ], BF16, 57344)
    rxs = aA("rxs", [128, SEQ], BF16, 59392)
    rt1 = aA("rt1", [128, SEQ], BF16, 61440)
    lnsc = aA("lnsc", [128, 2048], BF16, 57344)
    xntm = aA("xntm", [128, 2, 2048], BF16, 59392)
    dqw = aA("dqw", [128, 2, 2048], F32, 63488)
    dqvn = aA("dqvn", [128, 2048], F32, 71680)
    dqac = aA("dqac", [128, 2048], F16, 75776)
    dqtm = aA("dqtm", [128, 2048], F16, 77824)
    dqb = aA("dqb", [128, 2048], BF16, 79872)
    xin = aA("xin", [128, 2, 2048], F32, 81920)
    ssb = aA("ssb", [128, 2, 512], BF16, 81920)
    hact = aA("hact", [128, 86, 512], BF16, 0)
    xnh = aA("xnh", [128, DC, 512], BF16, 44032)
    dq1t = aA("dq1t", [128, 2, DC, 128], BF16, 60416)
    dq2t = aA("dq2t", [128, 2, DC, 128], BF16, 68608)
    dq3t = aA("dq3t", [128, 2, 1024], BF16, 76800)
    sgsb = aA("sgsb", [128, 1024], F32, 78848)
    sxsb = aA("sxsb", [128, 1024], BF16, 80896)
    xsl = aA("xsl", [128, 1024], F32, 81920)
    osb = aA("osb", [128, 2, 1024], F32, 86016)
    dqasm = aA("dqasm", [128, 1024], BF16, 92160)
    dqb1 = aA("dqb1", [128, 2048], BF16, 93184)
    dqam1 = aA("dqam1", [128, 32], F32, 95232)
    dqrc1 = aA("dqrc1", [128, 32], F32, 95296)
    fsc = aA("fsc", [128, 2], BF16, 95360)
    dqt2 = aA("dqt2", [128, 2048], F16, 90112)

    PB = nc.alloc_psum_tensor("PB", [128, 4, 1024], F32).ap()

    def pq(i):
        return PB[:, i, :]

    def pbk(q, half):
        return PB[:, q, 512 * half:512 * (half + 1)].bitcast(BF16)

    def sem(name):
        return Ctr(es.enter_context(nc.semaphore(name)))

    c_pe, c_dv, c_ac, c_cc = sem("c_pe"), sem("c_dv"), sem("c_ac"), sem("c_cc")
    s_ini = sem("s_ini")
    s_xin = [sem("s_xin0"), sem("s_xin1")]
    s_dqw = [sem("s_dqw0"), sem("s_dqw1")]
    s_wti = [sem("s_wti0"), sem("s_wti1")]
    s_m01 = sem("s_m01")
    s_qh = [sem("s_qh0"), sem("s_qh1")]
    s_kh = [sem("s_kh0"), sem("s_kh1")]
    s_att = sem("s_att")
    s_d12 = [sem("s_d12_0"), sem("s_d12_1")]
    s_d3 = [sem("s_d3_0"), sem("s_d3_1")]
    s_xnh = sem("s_xnh")
    s_sx5 = sem("s_sx5")
    s_wo = sem("s_wo")
    s_attf = sem("s_attf")
    s_wof = sem("s_wof")
    s_xnf = sem("s_xnf")
    s_wq = sem("s_wq")
    s_wk = sem("s_wk")
    s_wxn = sem("s_wxn")
    s_watt = sem("s_watt")
    s_ws = sem("s_ws")
    s_shw = {"w1": sem("s_sh1"), "w2": sem("s_sh2"), "w3": sem("s_sh3")}
    s_dxn = sem("s_dxn")
    s_out = sem("s_out")
    s_dbg = sem("s_dbg")

    block = es.enter_context(nc.Block())

    def mmu(t, o, l, r, start, stop):
        return t.matmul(o, l, r, start=start, stop=stop,
                        skip_group_check=True)

    g_trb = [Gate(), Gate()]
    tr_par = [0]
    g_tr3 = [Gate(), Gate()]
    tr3_par = [0]
    g_atb = Gate()

    MASK32 = []
    for i in range(0, 32, 2):
        MASK32 += [i + 1, i]

    with es:
        def _init(sy):
            s_ini.inc(sy.dma_start(out=idsb, in_=ident), 16)
            s_ini.inc(sy.dma_start(out=cqs[:, 0, :], in_=cosq), 16)
            s_ini.inc(sy.dma_start(out=cqs[:, 1, :], in_=sinq), 16)
            s_ini.inc(sy.dma_start(out=cks[:, 0, :], in_=cosk), 16)
            s_ini.inc(sy.dma_start(out=cks[:, 1, :], in_=sink), 16)
        block.sync(_init)
        INI = s_ini.v

        def _initdv(v):
            v.memset(cst[:, 0:1], 1e-5)
            c_dv.inc(v.memset(cst[:, 1:2], -0.5))
        block.vector(_initdv)
        EPS = cst[:, 0:1]
        MHALF = cst[:, 1:2]

        dq_units = ([("w1", t, s) for t in range(HT) for s in range(2)]
                    + [("w2", t, s) for t in range(HT) for s in range(2)]
                    + [("w3", t, 0) for t in range(DC)])
        g_dqw = [Gate(), Gate()]
        g_dqb = [Gate(), Gate()]
        g_dqasm = Gate()
        cc_done = {}
        dqst = {"i": 0}

        def dq_emit_unit():
            i = dqst["i"]
            if i >= len(dq_units):
                return
            dqst["i"] = i + 1
            w, t, s = dq_units[i]
            buf = i % 2
            if w == "w3":
                F = HSH
                src = w3s[t * 128:(t + 1) * 128, :]
                ntr = HT
            else:
                F = 2048
                src = (w1s if w == "w1" else w2s)[t * 128:(t + 1) * 128,
                                                  s * 2048:(s + 1) * 2048]
                ntr = 16
            nblk = F // 64
            wv = dqw[:, buf, 0:F]
            dqbu = dqb if buf == 0 else dqb1
            amt = dqam if buf == 0 else dqam1
            rct = dqrc if buf == 0 else dqrc1

            def _sp(sy, wv=wv, src=src, g=g_dqw[buf].snap()):
                g.wait(sy)
                s_dqw[buf].inc(sy.dma_start(out=wv, in_=src), 16)
            block.sync(_sp)
            ld = s_dqw[buf].v

            def _dv(v, wv=wv, F=F, nblk=nblk, buf=buf, ld=ld,
                    dqbu=dqbu, amt=amt, rct=rct, gb=g_dqb[buf].snap()):
                v.wait_ge(s_dqw[buf].h, ld)
                gb.wait(v)
                am = amt[:, 0:nblk]
                wv3 = wv.rearrange("p (b i) -> p b i", i=64)
                v.tensor_reduce(out=am, in_=wv3, axis=AX.X, op=OP.max,
                                apply_absolute_value=True)
                v.tensor_copy(cst[:, 3:4], cst[:, 0:1])
                v.tensor_copy(cst[:, 3:4], cst[:, 0:1])
                v.scalar_tensor_tensor(out=dqvn[:, 0:nblk], in0=am,
                                       scalar=0.0, in1=am,
                                       op0=OP.is_equal, op1=OP.add)
                v.tensor_copy(cst[:, 3:4], cst[:, 0:1])
                v.reciprocal(rct[:, 0:nblk], dqvn[:, 0:nblk])
                v.tensor_copy(cst[:, 3:4], cst[:, 0:1])
                rec_b = rct[:, 0:nblk].unsqueeze(2).to_broadcast(
                    [128, nblk, 64])
                v.scalar_tensor_tensor(
                    out=dqvn[:, 0:F].rearrange("p (b i) -> p b i", i=64),
                    in0=wv3, scalar=1.0, in1=rec_b, op0=OP.mult, op1=OP.mult)
                vn = dqvn[:, 0:F]
                v.tensor_scalar(out=dqac[:, 0:F], in0=vn,
                                scalar1=float(BOUNDS[0]),
                                scalar2=float(DELTAS[0]),
                                op0=OP.is_gt, op1=OP.mult)
                for j in range(1, 15):
                    tm = dqtm if j % 2 else dqt2
                    v.tensor_scalar(out=tm[:, 0:F], in0=vn,
                                    scalar1=float(BOUNDS[j]),
                                    scalar2=float(DELTAS[j]),
                                    op0=OP.is_gt, op1=OP.mult)
                    v.tensor_add(dqac[:, 0:F], dqac[:, 0:F], tm[:, 0:F])
                am_b = am.unsqueeze(2).to_broadcast([128, nblk, 64])
                v.scalar_tensor_tensor(
                    out=dqbu[:, 0:F].rearrange("p (b i) -> p b i", i=64),
                    in0=dqac[:, 0:F].rearrange("p (b i) -> p b i", i=64),
                    scalar=-1.0, in1=am_b, op0=OP.add, op1=OP.mult)
                c_dv.inc(v.tensor_copy(cst[:, 3:4], cst[:, 0:1]))
            block.vector(_dv)
            g_dqw[buf].set(c_dv, c_dv.v)
            dv_done = c_dv.v

            if DQ_MODE == "dve":
                return
            for g in range((ntr + 7) // 8):
                n_g = min(8, ntr - g * 8)
                tb = tr3_par[0]
                tr3_par[0] ^= 1
                bank = pbk(3, tb)

                def _pe(tt, g=g, n_g=n_g, bank=bank, gate=g_tr3[tb].snap(),
                        dv_done=dv_done, dqbu=dqbu):
                    tt.wait_ge(c_dv.h, dv_done)
                    gate.wait(tt)
                    tt.wait_ge(s_ini.h, INI)
                    for k in range(n_g):
                        j = g * 8 + k
                        ins = tt.transpose(bank[:, k * 128:(k + 1) * 128],
                                           dqbu[:, j * 128:(j + 1) * 128],
                                           idsb)
                    c_pe.inc(ins)
                block.tensor(_pe)
                pe_done = c_pe.v
                if g == (ntr + 7) // 8 - 1:
                    g_dqb[buf].set(c_pe, pe_done)
                if DQ_MODE == "tr":
                    g_tr3[tb].set(c_pe, pe_done)
                    continue

                def _ev(v, n_g=n_g, bank=bank, pe_done=pe_done,
                        gq=g_dqasm.snap(), ev_dst=dqasm):
                    v.wait_ge(c_pe.h, pe_done)
                    gq.wait(v)
                    v.tensor_copy(ev_dst[:, 0:n_g * 128], bank[:, 0:n_g * 128])
                    c_dv.inc(v.tensor_copy(cst[:, 3:4], cst[:, 0:1]))
                block.vector(_ev)
                if DQ_MODE in ("nogate", "nogate2"):
                    g_tr3[tb].set(c_pe, pe_done)
                else:
                    g_tr3[tb].set(c_dv, c_dv.v)
                ac_done = c_dv.v
                if DQ_MODE in ("trac", "nogate", "nogate2"):
                    g_dqasm.set(c_dv, ac_done)
                    continue

                def _sc(scq, w=w, t=t, s=s, g=g, n_g=n_g, ac_done=ac_done):
                    scq.wait_ge(c_dv.h, ac_done)
                    if w == "w3":
                        for k in range(n_g):
                            j = g * 8 + k
                            s_shw[w].inc(scq.dma_start(
                                out=sh["w3"][j * 128:(j + 1) * 128,
                                             t * 128:(t + 1) * 128],
                                in_=dqasm[:, k * 128:(k + 1) * 128]), 16)
                    else:
                        c0 = (s * 16 + g * 8) * 128
                        s_shw[w].inc(scq.dma_start(
                            out=sh[w][t * 128:(t + 1) * 128,
                                      c0:c0 + n_g * 128],
                            in_=dqasm[:, 0:n_g * 128]), 16)
                block.scalar(_sc)
                g_dqasm.set(s_shw[w], s_shw[w].v)

            if (w == "w1" and i == 21) or (w == "w2" and i == 43) \
                    or (w == "w3" and i == 75):
                shard_wr = s_shw[w].v

                def _pl(g_, w=w, shard_wr=shard_wr):
                    g_.wait_ge(s_shw[w].h, shard_wr)
                    c_cc.inc(g_.collective_compute(
                        "AllGather", OP.bypass,
                        replica_groups=[list(range(NC8))],
                        ins=[sh[w][:, :]], outs=[ff[w][:, :]]), 1)
                block.gpsimd(_pl)
                cc_done[w] = c_cc.v

        g_xnt_rd = Gate()
        g_xntm = Gate()
        g_xin = [Gate(), Gate()]
        g_xnt_wr = Gate()
        xin_cnt = [0]

        def p0_segment(seg_src, dq_slots=0, lnsc_gate=None):
            for j in range(8):
                rsrc = seg_src(j)
                xin_cnt[0] += 1
                nck = xin_cnt[0]
                lg = lnsc_gate if j == 0 else None

                def _sp(sy, rsrc=rsrc, g0=g_xin[0].snap(),
                        g1=g_xin[1].snap()):
                    g0.wait(sy)
                    s_xin[0].inc(sy.dma_start(out=xin[:, 0, :],
                                              in_=rsrc[:, 0:2048]), 16)
                    g1.wait(sy)
                    s_xin[1].inc(sy.dma_start(out=xin[:, 1, :],
                                              in_=rsrc[:, 2048:4096]), 16)
                block.sync(_sp)

                def _ac1(a, nck=nck, lg=lg):
                    if lg is not None:
                        lg.wait(a)
                    a.wait_ge(s_xin[0].h, 16 * nck)
                    a.activation(out=lnsc, in_=xin[:, 0, :],
                                 func=AF.Identity, bias=0.0, scale=1.0,
                                 accum_out=lnst[:, 0:1])
                    a.activation(out=lnsc, in_=xin[:, 0, :], func=AF.Square,
                                 bias=0.0, scale=1.0, accum_out=lnst[:, 1:2])
                    a.wait_ge(s_xin[1].h, 16 * nck)
                    a.activation(out=lnsc, in_=xin[:, 1, :],
                                 func=AF.Identity, bias=0.0, scale=1.0,
                                 accum_out=lnst[:, 2:3])
                    a.activation(out=lnsc, in_=xin[:, 1, :], func=AF.Square,
                                 bias=0.0, scale=1.0, accum_out=lnst[:, 3:4])
                    c_ac.inc(a.copy(lnsc[:, 0:4], lnst[:, 0:4]))
                block.scalar(_ac1)
                acv = c_ac.v

                def _dv1(v, acv=acv):
                    v.wait_ge(c_ac.h, acv)
                    v.tensor_add(lnst[:, 4:5], lnst[:, 0:1], lnst[:, 2:3])
                    v.tensor_add(lnst[:, 5:6], lnst[:, 1:2], lnst[:, 3:4])
                    v.tensor_scalar(out=lnst[:, 6:7], in0=lnst[:, 4:5],
                                    scalar1=1.0 / D, scalar2=None,
                                    op0=OP.mult)
                    v.tensor_scalar(out=lnst[:, 7:8], in0=lnst[:, 5:6],
                                    scalar1=1.0 / D, scalar2=None,
                                    op0=OP.mult)
                    v.tensor_mul(lnst[:, 8:9], lnst[:, 6:7], lnst[:, 6:7])
                    v.tensor_copy(cst[:, 3:4], cst[:, 0:1])
                    c_dv.inc(v.tensor_sub(lnst[:, 9:10], lnst[:, 7:8],
                                          lnst[:, 8:9]))
                block.vector(_dv1)
                dvv = c_dv.v

                def _ac2(a, dvv=dvv):
                    a.wait_ge(c_dv.h, dvv)
                    a.activation(out=lnst[:, 10:11], in_=lnst[:, 9:10],
                                 func=AF.Ln, bias=EPS, scale=1.0)
                    c_ac.inc(a.activation(out=lnst[:, 11:12],
                                          in_=lnst[:, 10:11],
                                          func=AF.Exp, bias=0.0,
                                          scale=MHALF))
                block.scalar(_ac2)
                acv2 = c_ac.v

                def _dv2(v, acv2=acv2, g=g_xntm.snap()):
                    v.wait_ge(c_ac.h, acv2)
                    g.wait(v)
                    v.scalar_tensor_tensor(out=lnst[:, 12:13],
                                           in0=lnst[:, 6:7], scalar=-1.0,
                                           in1=lnst[:, 11:12],
                                           op0=OP.mult, op1=OP.mult)
                    v.tensor_copy(cst[:, 3:4], cst[:, 0:1])
                    v.tensor_scalar(out=xntm[:, 0, :], in0=xin[:, 0, :],
                                    scalar1=lnst[:, 11:12],
                                    scalar2=lnst[:, 12:13],
                                    op0=OP.mult, op1=OP.add)
                    v.tensor_scalar(out=xntm[:, 1, :], in0=xin[:, 1, :],
                                    scalar1=lnst[:, 11:12],
                                    scalar2=lnst[:, 12:13],
                                    op0=OP.mult, op1=OP.add)
                    c_dv.inc(v.tensor_copy(cst[:, 3:4], cst[:, 0:1]))
                block.vector(_dv2)
                g_xin[0].set(c_dv, c_dv.v)
                g_xin[1].set(c_dv, c_dv.v)
                nrm = c_dv.v

                for g in range(4):
                    tb = tr_par[0]
                    tr_par[0] ^= 1
                    bank = pbk(2, tb)

                    def _pe(tt, g=g, bank=bank, gate=g_trb[tb].snap(),
                            gx=g_xnt_rd.snap(), gw=g_xnt_wr.snap(),
                            nrm=nrm):
                        tt.wait_ge(c_dv.h, nrm)
                        gate.wait(tt)
                        gx.wait(tt)
                        gw.wait(tt)
                        tt.wait_ge(s_ini.h, INI)
                        for k in range(8):
                            dc = g * 8 + k
                            xsrc = xntm[:, dc // 16,
                                        (dc % 16) * 128:(dc % 16) * 128
                                        + 128]
                            ins = tt.transpose(
                                bank[:, k * 128:(k + 1) * 128], xsrc, idsb)
                        c_pe.inc(ins)
                    block.tensor(_pe)
                    pev = c_pe.v
                    if g == 3:
                        g_xntm.set(c_pe, c_pe.v)

                    def _ac3(a, g=g, j=j, bank=bank, pev=pev):
                        a.wait_ge(c_pe.h, pev)
                        dst = xnt[:, g * 8:(g + 1) * 8,
                                  j * 128:(j + 1) * 128]
                        a.copy(dst, bank.rearrange("p (a q) -> p a q",
                                                   q=128))
                        c_ac.inc(a.copy(cst[:, 2:3], cst[:, 0:1]))
                    block.scalar(_ac3)
                    g_trb[tb].set(c_ac, c_ac.v)
                for _ in range(dq_slots):
                    dq_emit_unit()
            return c_ac.v

        g_wti = [Gate(), Gate()]
        g_psq = [Gate(), Gate()]
        g_rxq = Gate()
        wti_cnt = [0, 0]

        def p2_pass(cc_list, xnt_ready_ac, rope_tab, half, dq_every):
            for idx, cc in enumerate(cc_list):
                bufw = idx % 2
                wti_cnt[bufw] += 1
                nld = wti_cnt[bufw]

                def _pl(g_, cc=cc, bufw=bufw, g=g_wti[bufw].snap()):
                    g.wait(g_)
                    s_wti[bufw].inc(g_.dma_start(out=wti[:, bufw, :, :],
                                                 in_=wqkv_t[cc]), 16)
                block.gpsimd(_pl)

                bufp = idx % 2

                def _pe(tt, bufw=bufw, bufp=bufp, nld=nld,
                        g=g_psq[bufp].snap(), xr=xnt_ready_ac):
                    tt.wait_ge(s_wti[bufw].h, 16 * nld)
                    tt.wait_ge(c_ac.h, xr)
                    g.wait(tt)
                    for dc in range(DC):
                        st = dc == 0
                        sp_ = dc == DC - 1
                        l = wti[:, bufw, dc, :]
                        r = xnt[:, dc, :]
                        mmu(tt, pq(bufp)[:, 0:512], l, r[:, 0:512], st, sp_)
                        ins = mmu(tt, pq(bufp)[:, 512:1024], l,
                                  r[:, 512:1024], st, sp_)
                    c_pe.inc(ins)
                block.tensor(_pe)
                g_wti[bufw].set(c_pe, c_pe.v)
                g_xnt_rd.set(c_pe, c_pe.v)
                pev = c_pe.v

                def _acc(a, bufp=bufp, pev=pev, g=g_rxq.snap()):
                    a.wait_ge(c_pe.h, pev)
                    g.wait(a)
                    a.copy(rxq[:, 0:TOK], pq(bufp)[:, 0:TOK])
                    c_ac.inc(a.copy(cst[:, 2:3], cst[:, 0:1]))
                block.scalar(_acc)
                g_psq[bufp].set(c_ac, c_ac.v)
                acv = c_ac.v

                if rope_tab is not None:
                    toff = 0 if cc < 32 else half * 1024

                    def _dv(v, ctab=rope_tab, toff=toff, acv=acv):
                        v.wait_ge(c_ac.h, acv)
                        v.wait_ge(s_ini.h, INI)
                        v.stream_shuffle(rxs[:, 0:TOK], rxq[:, 0:TOK],
                                         MASK32)
                        v.tensor_mul(rt1[:, 0:TOK], rxq[:, 0:TOK],
                                     ctab[:, 0, toff:toff + TOK])
                        v.tensor_mul(rxs[:, 0:TOK], rxs[:, 0:TOK],
                                     ctab[:, 1, toff:toff + TOK])
                        c_dv.inc(v.tensor_add(rxq[:, 0:TOK], rt1[:, 0:TOK],
                                              rxs[:, 0:TOK]))
                    block.vector(_dv)
                    dvv = c_dv.v

                    def _sc(scq, cc=cc, dvv=dvv, half=half):
                        scq.wait_ge(c_dv.h, dvv)
                        if cc < 32:
                            s_wq.inc(scq.dma_start(out=q_hbm[cc],
                                                   in_=rxq[:, 0:TOK]), 16)
                        else:
                            dst = k_hbm[cc - 32][:, half * 1024:
                                                 half * 1024 + 1024]
                            s_wk.inc(scq.dma_start(out=dst,
                                                   in_=rxq[:, 0:TOK]), 16)
                    block.scalar(_sc)
                    g_rxq.set(s_wq if cc < 32 else s_wk,
                              (s_wq if cc < 32 else s_wk).v)
                else:
                    tb = tr_par[0]
                    tr_par[0] ^= 1
                    bank = pbk(2, tb)

                    def _pe2(tt, bank=bank, gate=g_trb[tb].snap(), acv=acv):
                        tt.wait_ge(c_ac.h, acv)
                        gate.wait(tt)
                        for k in range(8):
                            ins = tt.transpose(
                                bank[:, k * 128:(k + 1) * 128],
                                rxq[:, k * 128:(k + 1) * 128], idsb)
                        c_pe.inc(ins)
                    block.tensor(_pe2)
                    pev2 = c_pe.v
                    g_rxq.set(c_pe, c_pe.v)

                    def _ac2(a, cc=cc, bank=bank, pev2=pev2, half=half):
                        a.wait_ge(c_pe.h, pev2)
                        dst = vsb[:, half * 8:half * 8 + 8,
                                  (cc - 40) * 128:(cc - 40) * 128 + 128]
                        a.copy(dst, bank.rearrange("p (a q) -> p a q",
                                                   q=128))
                        c_ac.inc(a.copy(cst[:, 2:3], cst[:, 0:1]))
                    block.scalar(_ac2)
                    g_trb[tb].set(c_ac, c_ac.v)
                if (idx + 1) % dq_every == 0:
                    dq_emit_unit()

        own_done = p0_segment(lambda j: x_own[j * 128:(j + 1) * 128, :])
        p2_pass(list(range(32)), own_done, cqs, 0, dq_every=2)

        xnrd = g_xnt_rd.val

        def _scxn(scq, xnrd=xnrd, od=own_done):
            scq.wait_ge(c_ac.h, od)
            scq.wait_ge(c_pe.h, xnrd)
            s_wxn.inc(scq.dma_start(out=xn_own_hbm[:, :, :],
                                    in_=xnt[:, :, :]), 16)
        block.scalar(_scxn)
        g_xnt_wr.set(s_wxn, s_wxn.v)

        ctxA_done = p0_segment(
            lambda j: x_ctx[j * 128:(j + 1) * 128, :], dq_slots=1,
            lnsc_gate=g_rxq.snap())
        if dbg:
            def _scdA(scq, cd=ctxA_done):
                scq.wait_ge(c_ac.h, cd)
                s_dxn.inc(scq.dma_start(out=xnA_hbm[:, :, :],
                                        in_=xnt[:, :, :]), 16)
            block.scalar(_scdA)
            g_xnt_wr.set(s_dxn, s_dxn.v)
        else:
            g_xnt_wr.set(None, 0)
        p2_pass(list(range(32, 40)), ctxA_done, cks, 0, dq_every=1)
        p2_pass(list(range(40, 48)), ctxA_done, None, 0, dq_every=1)
        ctxB_done = p0_segment(
            lambda j: x_ctx[1024 + j * 128:1024 + (j + 1) * 128, :],
            dq_slots=1, lnsc_gate=g_rxq.snap())
        if dbg:
            def _scdB(scq, cd=ctxB_done):
                scq.wait_ge(c_ac.h, cd)
                s_dxn.inc(scq.dma_start(out=xnB_hbm[:, :, :],
                                        in_=xnt[:, :, :]), 16)
            block.scalar(_scdB)
        p2_pass(list(range(32, 40)), ctxB_done, cks, 1, dq_every=1)
        p2_pass(list(range(40, 48)), ctxB_done, None, 1, dq_every=1)

        kv_done_pe = c_pe.v
        dxn_fin = s_dxn.v

        def _spm(sy):
            sy.wait_ge(c_pe.h, kv_done_pe)
            if dbg:
                sy.wait_ge(s_dxn.h, dxn_fin)
            s_m01.inc(sy.dma_start(out=m01[:, :, :], in_=m01i), 16)
        block.sync(_spm)

        g_qh = [Gate(c_pe, kv_done_pe), Gate(c_pe, kv_done_pe)]
        g_kh = [Gate(c_pe, kv_done_pe), Gate(c_pe, kv_done_pe)]
        g_scq = [Gate(), Gate()]
        g_pbuf = [Gate(s_dxn, dxn_fin) if dbg else Gate(),
                  Gate(s_dxn, dxn_fin) if dbg else Gate()]
        g_ptb = [Gate(), Gate()]
        g_avb = Gate()
        WQ_ALL = s_wq.v
        WK_ALL = s_wk.v
        qh_cnt = [0, 0]
        kh_cnt = [0, 0]
        kld_cur = [0]
        for h in range(NH):
            g = h // 4
            bq = h % 2
            bg = g % 2
            qh_cnt[bq] += 1
            nq = qh_cnt[bq]

            def _spq(sy, h=h, bq=bq, gg=g_qh[bq].snap()):
                gg.wait(sy)
                sy.wait_ge(s_wq.h, WQ_ALL)
                s_qh[bq].inc(sy.dma_start(out=qh[:, bq, :],
                                          in_=q_hbm[h]), 16)
            block.sync(_spq)
            if h % 4 == 0:
                kh_cnt[bg] += 1

                def _spk(sy, g=g, bg=bg, gg=g_kh[bg].snap()):
                    gg.wait(sy)
                    sy.wait_ge(s_wk.h, WK_ALL)
                    s_kh[bg].inc(sy.dma_start(out=khh[:, bg, :],
                                              in_=k_hbm[g]), 16)
                block.sync(_spk)
            nk = kh_cnt[bg]

            for qc in range(8):
                def _pes(tt, qc=qc, bq=bq, bg=bg, nq=nq, nk=nk,
                         w0=g_scq[0].snap(), w1=g_scq[1].snap()):
                    tt.wait_ge(s_qh[bq].h, 16 * nq)
                    tt.wait_ge(s_kh[bg].h, 16 * nk)
                    lhs = qh[:, bq, qc * 128:(qc + 1) * 128]
                    w0.wait(tt)
                    mmu(tt, pq(0)[:, 0:512], lhs, khh[:, bg, 0:512],
                        True, True)
                    c_pe.inc(mmu(tt, pq(0)[:, 512:1024], lhs,
                                 khh[:, bg, 512:1024], True, True))
                    w1.wait(tt)
                    mmu(tt, pq(1)[:, 0:512], lhs, khh[:, bg, 1024:1536],
                        True, True)
                    c_pe.inc(mmu(tt, pq(1)[:, 512:1024], lhs,
                                 khh[:, bg, 1536:2048], True, True))
                block.tensor(_pes)
                pe_s0 = c_pe.v - 1
                pe_s1 = c_pe.v

                def _dvm(v, pe_s0=pe_s0, pe_s1=pe_s1):
                    v.wait_ge(c_pe.h, pe_s0)
                    v.reduce_max(out=sst[:, 0:1], in_=pq(0), axis=AX.X)
                    v.wait_ge(c_pe.h, pe_s1)
                    v.reduce_max(out=sst[:, 1:2], in_=pq(1), axis=AX.X)
                    v.tensor_copy(cst[:, 3:4], cst[:, 0:1])
                    v.tensor_max(sst[:, 2:3], sst[:, 0:1], sst[:, 1:2])
                    v.tensor_copy(cst[:, 3:4], cst[:, 0:1])
                    c_dv.inc(v.tensor_scalar(out=sst[:, 3:4],
                                             in0=sst[:, 2:3], scalar1=-1.0,
                                             scalar2=None, op0=OP.mult))
                block.vector(_dvm)
                dv_m = c_dv.v

                bp = qc % 2

                def _ace(a, bp=bp, dv_m=dv_m, g=g_pbuf[bp].snap()):
                    a.wait_ge(c_dv.h, dv_m)
                    g.wait(a)
                    a.activation(out=psb[:, bp, 0:1024], in_=pq(0),
                                 func=AF.Exp, bias=sst[:, 3:4], scale=1.0)
                    a.activation(out=psb[:, bp, 1024:2048], in_=pq(1),
                                 func=AF.Exp, bias=sst[:, 3:4], scale=1.0)
                    c_ac.inc(a.copy(cst[:, 2:3], cst[:, 0:1]))
                block.scalar(_ace)
                g_scq[0].set(c_ac, c_ac.v)
                g_scq[1].set(c_ac, c_ac.v)
                ac_e = c_ac.v

                def _dvk(v, bp=bp, qc=qc, ac_e=ac_e):
                    v.wait_ge(c_ac.h, ac_e)
                    v.wait_ge(s_m01.h, 16)
                    v.scalar_tensor_tensor(
                        out=psb[:, bp, :], in0=psb[:, bp, :], scalar=1.0,
                        in1=m01[:, qc, :], op0=OP.mult, op1=OP.mult,
                        accum_out=sst[:, 4:5])
                    v.tensor_copy(cst[:, 3:4], cst[:, 0:1])
                    v.reciprocal(sst[:, 5:6], sst[:, 4:5])
                    c_dv.inc(v.tensor_copy(cst[:, 3:4], cst[:, 0:1]))
                block.vector(_dvk)
                dv_k = c_dv.v

                for gg in range(2):
                    tb = tr_par[0]
                    tr_par[0] ^= 1
                    bank = pbk(2, tb)

                    def _pet(tt, gg=gg, bp=bp, bank=bank,
                             gate=g_trb[tb].snap(), dv_k=dv_k,
                             gp=g_ptb[bp].snap()):
                        tt.wait_ge(c_dv.h, dv_k)
                        gate.wait(tt)
                        if gg == 0:
                            gp.wait(tt)
                        for k in range(8):
                            tkc = gg * 8 + k
                            ins = tt.transpose(
                                bank[:, k * 128:(k + 1) * 128],
                                psb[:, bp, tkc * 128:(tkc + 1) * 128],
                                idsb)
                        c_pe.inc(ins)
                    block.tensor(_pet)
                    pe_t = c_pe.v

                    def _act(a, gg=gg, bp=bp, bank=bank, pe_t=pe_t):
                        a.wait_ge(c_pe.h, pe_t)
                        a.copy(ptsb[:, bp, gg * 1024:(gg + 1) * 1024], bank)
                        c_ac.inc(a.copy(cst[:, 2:3], cst[:, 0:1]))
                    block.scalar(_act)
                    g_trb[tb].set(c_ac, c_ac.v)
                g_pbuf[bp].set(c_pe, c_pe.v)
                pt_rdy = c_ac.v

                def _pav(tt, bp=bp, g=g, pt_rdy=pt_rdy,
                         ga=g_avb.snap()):
                    tt.wait_ge(c_ac.h, pt_rdy)
                    ga.wait(tt)
                    for tkc in range(16):
                        ins = mmu(tt, PB[:, 3, 0:128],
                                  ptsb[:, bp, tkc * 128:(tkc + 1) * 128],
                                  vsb[:, tkc, g * 128:(g + 1) * 128],
                                  tkc == 0, tkc == 15)
                    c_pe.inc(ins)
                block.tensor(_pav)
                g_ptb[bp].set(c_pe, c_pe.v)
                pe_av = c_pe.v

                def _dva(v, pe_av=pe_av):
                    v.wait_ge(c_pe.h, pe_av)
                    v.tensor_scalar(out=avsb, in0=PB[:, 3, 0:128],
                                    scalar1=sst[:, 5:6], scalar2=None,
                                    op0=OP.mult)
                    c_dv.inc(v.tensor_copy(cst[:, 3:4], cst[:, 0:1]))
                block.vector(_dva)
                g_avb.set(c_dv, c_dv.v)
                dv_a = c_dv.v

                def _pat(tt, qc=qc, dv_a=dv_a, ga=g_atb.snap()):
                    tt.wait_ge(c_dv.h, dv_a)
                    if qc == 0:
                        ga.wait(tt)
                    c_pe.inc(tt.transpose(
                        pbk(3, 1)[:, qc * 128:(qc + 1) * 128], avsb, idsb))
                block.tensor(_pat)
            pe_t8 = c_pe.v

            def _aca(a, pe_t8=pe_t8):
                a.wait_ge(c_pe.h, pe_t8)
                a.copy(at_h, pbk(3, 1))
                c_ac.inc(a.copy(cst[:, 2:3], cst[:, 0:1]))
            block.scalar(_aca)
            g_atb.set(c_ac, c_ac.v)
            ac_at = c_ac.v

            def _sca(scq, h=h, ac_at=ac_at):
                scq.wait_ge(c_ac.h, ac_at)
                s_watt.inc(scq.dma_start(out=attT_hbm[:, h, :],
                                         in_=at_h), 16)
            block.scalar(_sca)
            g_qh[bq].set(c_pe, c_pe.v)
            g_kh[bg].set(c_pe, c_pe.v)

        while dqst["i"] < len(dq_units):
            dq_emit_unit()

        g_tr3[0].set(c_ac, c_ac.v)
        g_tr3[1].set(c_ac, c_ac.v)
        p3_ac = c_ac.v
        p3_dv = c_dv.v
        p3_pe = c_pe.v
        WATT_ALL = s_watt.v

        if dbg:
            def _scdV(scq, p3_pe=p3_pe):
                scq.wait_ge(c_pe.h, p3_pe)
                s_dxn.inc(scq.dma_start(out=v_hbm[:, :, :],
                                        in_=vsb[:, :, :]), 16)
            block.scalar(_scdV)
        dxnv_fin = s_dxn.v

        def _spat(sy):
            sy.wait_ge(c_ac.h, p3_ac)
            sy.wait_ge(c_dv.h, p3_dv)
            sy.wait_ge(c_pe.h, p3_pe)
            sy.wait_ge(s_watt.h, WATT_ALL)
            s_att.inc(sy.dma_start(out=attT, in_=attT_hbm), 16)
            s_attf.inc(sy.dma_start(out=fsc, in_=ident[:, 0:2]), 16)
        block.sync(_spat)

        g_wo = Gate(c_pe, p3_pe)
        if dbg:
            g_wo = Gate(s_dxn, dxnv_fin)
        drain_dv = c_dv.v
        drain_cc = c_cc.v
        g_ps4 = [Gate(), Gate()]
        g_ssb = [Gate(), Gate()]
        for e8 in range(8):
            def _plw(g_, e8=e8, gg=g_wo.snap(), p3_pe=p3_pe):
                gg.wait(g_)
                g_.wait_ge(c_pe.h, p3_pe)
                if e8 == 0 and DQ_MODE == "serp4":
                    g_.wait_ge(c_dv.h, drain_dv)
                    g_.wait_ge(c_cc.h, drain_cc)
                s_wo.inc(g_.dma_start(out=wosb, in_=w_ob[e8]), 16)
                s_wof.inc(g_.dma_start(out=fsc, in_=ident[:, 0:2]), 16)
            block.gpsimd(_plw)
            nwo = s_wof.v
            for tc in range(8):
                bp = tc % 2

                def _pe4(tt, tc=tc, bp=bp, nwo=nwo,
                         gg=g_ps4[bp].snap()):
                    tt.wait_ge(s_wof.h, nwo)
                    tt.wait_ge(s_attf.h, 16)
                    gg.wait(tt)
                    for vfc in range(DC):
                        ins = mmu(tt, pq(bp)[:, 0:512],
                                  attT[:, vfc, tc * 128:(tc + 1) * 128],
                                  wosb[:, vfc, :], vfc == 0, vfc == DC - 1)
                    c_pe.inc(ins)
                block.tensor(_pe4)
                pev = c_pe.v

                def _ac4(a, bp=bp, pev=pev, gg=g_ssb[bp].snap()):
                    a.wait_ge(c_pe.h, pev)
                    gg.wait(a)
                    a.copy(ssb[:, bp, :], pq(bp)[:, 0:512])
                    c_ac.inc(a.copy(cst[:, 2:3], cst[:, 0:1]))
                block.scalar(_ac4)
                g_ps4[bp].set(c_ac, c_ac.v)
                dvv = c_ac.v

                def _sc4(scq, e8=e8, tc=tc, bp=bp, dvv=dvv):
                    scq.wait_ge(c_ac.h, dvv)
                    s_ws.inc(scq.dma_start(
                        out=s_hbm[tc][:, e8 * 512:(e8 + 1) * 512],
                        in_=ssb[:, bp, :]), 16)
                block.scalar(_sc4)
                g_ssb[bp].set(s_ws, s_ws.v)
            g_wo.set(c_pe, c_pe.v)
            dq_emit_unit()
            dq_emit_unit()
        while dqst["i"] < len(dq_units):
            dq_emit_unit()

        p4_pe = c_pe.v
        p4_dv = c_dv.v
        WS_ALL = s_ws.v
        at2_fin = s_dbg.v
        if d_at2 is not None:
            def _scat2(scq, p4_pe=p4_pe):
                scq.wait_ge(c_pe.h, p4_pe)
                s_dbg.inc(scq.dma_start(
                    out=d_at2.rearrange("p a t -> p (a t)"),
                    in_=attT.rearrange("p a t -> p (a t)")), 16)
            block.scalar(_scat2)
            at2_fin = s_dbg.v
        g1cc = cc_done.get("w1", c_cc.v)
        g2cc = cc_done.get("w2", c_cc.v)
        g3cc = cc_done.get("w3", c_cc.v)

        g_xnh = Gate(c_pe, p4_pe)
        g_d12 = [Gate(c_pe, p4_pe), Gate(c_pe, p4_pe)]
        g_gq = [Gate(c_dv, p4_dv), Gate(c_dv, p4_dv)]
        g_uq = [Gate(c_dv, p4_dv), Gate(c_dv, p4_dv)]
        g_sg5 = Gate()
        g_d3 = [Gate(c_pe, p4_pe), Gate(c_pe, p4_pe)]
        g_dwn = Gate()
        g_sx5 = Gate()
        g_osb = [Gate(), Gate()]
        d12_cnt = [0, 0]
        d3_cnt = [0, 0]
        sx5_cnt = [0]

        for hf in range(2):
            def _spxn(sy, hf=hf, gg=g_xnh.snap()):
                gg.wait(sy)
                sy.wait_ge(c_dv.h, p4_dv)
                if at2_fin:
                    sy.wait_ge(s_dbg.h, at2_fin)
                sy.wait_ge(s_wxn.h, 16)
                src = xn_own_hbm[:, :, hf * 512:(hf + 1) * 512]
                s_xnh.inc(sy.dma_start(out=xnh, in_=src), 16)
                s_xnf.inc(sy.dma_start(out=fsc, in_=ident[:, 0:2]), 16)
            block.sync(_spxn)
            nxn = s_xnf.v

            for hi, (r, ht) in enumerate(HC_LIST):
                row0 = r * HSH + ht * 128
                bufw = hi % 2
                d12_cnt[bufw] += 1
                nld = d12_cnt[bufw]

                def _spd(sy, row0=row0, bufw=bufw,
                         gg=g_d12[bufw].snap()):
                    gg.wait(sy)
                    sy.wait_ge(c_dv.h, p4_dv)
                    if at2_fin:
                        sy.wait_ge(s_dbg.h, at2_fin)
                    sy.wait_ge(c_cc.h, g1cc)
                    s_d12[bufw].inc(sy.dma_start(
                        out=dq1t[:, bufw, :, :].rearrange(
                            "p a q -> p (a q)"),
                        in_=ff["w1"][row0:row0 + 128, :]), 16)
                    sy.wait_ge(c_cc.h, g2cc)
                    s_d12[bufw].inc(sy.dma_start(
                        out=dq2t[:, bufw, :, :].rearrange(
                            "p a q -> p (a q)"),
                        in_=ff["w2"][row0:row0 + 128, :]), 16)
                block.sync(_spd)

                bb = hi % 2

                def _pe5(tt, bufw=bufw, bb=bb, nld=nld, nxn=nxn,
                         wg=g_gq[bb].snap(), wu=g_uq[bb].snap()):
                    tt.wait_ge(s_d12[bufw].h, 32 * nld)
                    tt.wait_ge(s_xnf.h, nxn)
                    wg.wait(tt)
                    for dc in range(DC):
                        ins = mmu(tt, PB[:, bb, 0:512],
                                  dq1t[:, bufw, dc, :], xnh[:, dc, :],
                                  dc == 0, dc == DC - 1)
                    c_pe.inc(ins)
                    wu.wait(tt)
                    for dc in range(DC):
                        ins = mmu(tt, PB[:, 2 + bb, 0:512],
                                  dq2t[:, bufw, dc, :], xnh[:, dc, :],
                                  dc == 0, dc == DC - 1)
                    c_pe.inc(ins)
                block.tensor(_pe5)
                g_d12[bufw].set(c_pe, c_pe.v)
                pe_g = c_pe.v - 1
                pe_u = c_pe.v

                def _ac5(a, bb=bb, pe_g=pe_g, gg=g_sg5.snap()):
                    a.wait_ge(c_pe.h, pe_g)
                    gg.wait(a)
                    a.activation(out=sgsb[:, 0:512], in_=PB[:, bb, 0:512],
                                 func=AF.Silu, bias=0.0, scale=1.0)
                    c_ac.inc(a.copy(cst[:, 2:3], cst[:, 0:1]))
                block.scalar(_ac5)
                g_gq[bb].set(c_ac, c_ac.v)
                ac_g = c_ac.v

                def _dv5(v, hi=hi, bb=bb, pe_u=pe_u, ac_g=ac_g):
                    v.wait_ge(c_pe.h, pe_u)
                    v.wait_ge(c_ac.h, ac_g)
                    v.scalar_tensor_tensor(
                        out=hact[:, hi, :], in0=PB[:, 2 + bb, 0:512],
                        scalar=1.0, in1=sgsb[:, 0:512],
                        op0=OP.mult, op1=OP.mult)
                    c_dv.inc(v.tensor_copy(cst[:, 3:4], cst[:, 0:1]))
                block.vector(_dv5)
                g_uq[bb].set(c_dv, c_dv.v)
                g_sg5.set(c_dv, c_dv.v)
            gu_dv = c_dv.v

            for dsp in range(4):
                for hi, (r, ht) in enumerate(HC_LIST):
                    row0 = r * HSH + ht * 128
                    bufw = hi % 2
                    d3_cnt[bufw] += 1
                    n3 = d3_cnt[bufw]

                    def _spd3(sy, row0=row0, bufw=bufw, dsp=dsp,
                              gg=g_d3[bufw].snap()):
                        gg.wait(sy)
                        sy.wait_ge(c_dv.h, p4_dv)
                        sy.wait_ge(c_cc.h, g3cc)
                        s_d3[bufw].inc(sy.dma_start(
                            out=dq3t[:, bufw, :],
                            in_=ff["w3"][row0:row0 + 128,
                                         dsp * 1024:(dsp + 1) * 1024]), 16)
                    block.sync(_spd3)

                    def _pe6(tt, hi=hi, bufw=bufw, n3=n3,
                             gg=g_dwn.snap(), gu_dv=gu_dv):
                        tt.wait_ge(s_d3[bufw].h, 16 * n3)
                        if hi == 0:
                            tt.wait_ge(c_dv.h, gu_dv)
                            gg.wait(tt)
                        for tc in range(4):
                            l = hact[:, hi, tc * 128:(tc + 1) * 128]
                            mmu(tt, PB[:, tc, 0:512], l,
                                dq3t[:, bufw, 0:512], hi == 0, hi == 85)
                            ins = mmu(tt, PB[:, tc, 512:1024], l,
                                      dq3t[:, bufw, 512:1024],
                                      hi == 0, hi == 85)
                        c_pe.inc(ins)
                    block.tensor(_pe6)
                    g_d3[bufw].set(c_pe, c_pe.v)
                pe_dn = c_pe.v

                for tc in range(4):
                    def _ac6(a, tc=tc, pe_dn=pe_dn, gg=g_sg5.snap()):
                        a.wait_ge(c_pe.h, pe_dn)
                        gg.wait(a)
                        a.activation(out=sgsb, in_=PB[:, tc, :],
                                     func=AF.Silu, bias=0.0, scale=1.0)
                        c_ac.inc(a.copy(cst[:, 2:3], cst[:, 0:1]))
                    block.scalar(_ac6)
                    ac_sl = c_ac.v

                    sx5_cnt[0] += 1
                    nsx = sx5_cnt[0]

                    def _spl(sy, hf=hf, tc=tc, dsp=dsp,
                             gg=g_sx5.snap()):
                        gg.wait(sy)
                        sy.wait_ge(s_ws.h, WS_ALL)
                        s_sx5.inc(sy.dma_start(
                            out=sxsb,
                            in_=s_hbm[hf * 4 + tc][:, dsp * 1024:
                                                   (dsp + 1) * 1024]), 16)
                        r0 = hf * 512 + tc * 128
                        s_sx5.inc(sy.dma_start(
                            out=xsl,
                            in_=x_own[r0:r0 + 128,
                                      dsp * 1024:(dsp + 1) * 1024]), 16)
                    block.sync(_spl)

                    bo = tc % 2

                    def _dv6(v, bo=bo, ac_sl=ac_sl, nsx=nsx,
                             gg=g_osb[bo].snap()):
                        v.wait_ge(c_ac.h, ac_sl)
                        v.wait_ge(s_sx5.h, 32 * nsx)
                        gg.wait(v)
                        v.scalar_tensor_tensor(
                            out=osb[:, bo, :], in0=sxsb, scalar=1.0,
                            in1=sgsb, op0=OP.mult, op1=OP.mult)
                        v.tensor_add(osb[:, bo, :], osb[:, bo, :], xsl)
                        c_dv.inc(v.tensor_copy(cst[:, 3:4], cst[:, 0:1]))
                    block.vector(_dv6)
                    g_sx5.set(c_dv, c_dv.v)
                    g_sg5.set(c_dv, c_dv.v)
                    dv_o = c_dv.v

                    def _spo(sy, hf=hf, tc=tc, dsp=dsp, bo=bo, dv_o=dv_o):
                        sy.wait_ge(c_dv.h, dv_o)
                        r0 = hf * 512 + tc * 128
                        s_out.inc(sy.dma_start(
                            out=outt[r0:r0 + 128,
                                     dsp * 1024:(dsp + 1) * 1024],
                            in_=osb[:, bo, :]), 16)
                    block.sync(_spo)
                    g_osb[bo].set(s_out, s_out.v)
                g_dwn.set(c_ac, c_ac.v)
            g_xnh.set(c_pe, c_pe.v)

        fin_cc = c_cc.v
        if dbg:
            OUT_ALL0 = s_out.v

            def _pldbg(g_):
                g_.wait_ge(s_out.h, OUT_ALL0)
                g_.wait_ge(s_dxn.h, s_dxn.v)
                for nm, (dst, srct) in dbg.items():
                    d = dst.rearrange("a p t -> (a p) t") \
                        if len(dst.shape) == 3 else dst
                    s2 = srct.rearrange("a p t -> (a p) t") \
                        if len(srct.shape) == 3 else srct
                    s_dbg.inc(g_.dma_start(out=d, in_=s2), 16)
            block.gpsimd(_pldbg)

        OUT_ALL = s_out.v
        DBG_ALL = s_dbg.v

        def _fin_sp(sy):
            sy.wait_ge(s_out.h, OUT_ALL)
        block.sync(_fin_sp)

        def _fin_pl(g_):
            g_.wait_ge(s_dbg.h, DBG_ALL)
            g_.wait_ge(c_cc.h, fin_cc)
        block.gpsimd(_fin_pl)

    return nc



def _stage_inputs(x, w_qkv, w_o, w1, w2, w3):
    f32 = np.float32
    bf16 = ml_dtypes.bfloat16
    x = np.asarray(x, dtype=f32)
    wqkv_t = np.ascontiguousarray(
        np.asarray(w_qkv, dtype=f32).reshape(DC, 128, 48, 128)
        .transpose(2, 1, 0, 3))
    w_ob = np.ascontiguousarray(
        np.asarray(w_o, dtype=f32).reshape(DC, 128, 8, 512)
        .transpose(2, 1, 0, 3))
    w1p = np.zeros((HPAD, D), dtype=f32)
    w1p[:HFF] = np.asarray(w1, dtype=f32)
    w2p = np.zeros((HPAD, D), dtype=f32)
    w2p[:HFF] = np.asarray(w2, dtype=f32)
    w3p = np.zeros((D, HPAD), dtype=f32)
    w3p[:, :HFF] = np.asarray(w3, dtype=f32)
    ident = np.eye(128, dtype=f32).astype(bf16)

    def tables(pos, scale):
        ang = (pos[None, :].astype(f32) * ROPE_INVF[:, None]).astype(f32)
        cos = np.cos(ang).astype(f32) * scale
        sin = np.sin(ang).astype(f32) * scale
        C = np.repeat(cos, 2, axis=0)
        S = np.repeat(sin, 2, axis=0)
        S[0::2] *= -1.0
        return (np.ascontiguousarray(C.astype(bf16)),
                np.ascontiguousarray(S.astype(bf16)))

    in_maps = []
    for c in range(NC8):
        b, half = c // 2, c % 2
        q_off = half * TOK
        cq, sq = tables(q_off + np.arange(TOK), 1.0 / np.sqrt(128.0))
        ck, sk = tables(np.arange(SEQ), 1.0)
        kidx = np.arange(SEQ)[None, :]
        qpos = (q_off + np.arange(TOK))[:, None]
        m01 = np.ascontiguousarray(
            (kidx <= qpos).astype(f32).astype(bf16)
            .reshape(8, 128, SEQ).transpose(1, 0, 2))
        in_maps.append({
            "x_own": np.ascontiguousarray(x[b, q_off:q_off + TOK]),
            "x_ctx": np.ascontiguousarray(x[b]),
            "wqkv_t": wqkv_t,
            "w_ob": w_ob,
            "w1s": np.ascontiguousarray(w1p[c * HSH:(c + 1) * HSH]),
            "w2s": np.ascontiguousarray(w2p[c * HSH:(c + 1) * HSH]),
            "w3s": np.ascontiguousarray(w3p[:, c * HSH:(c + 1) * HSH]),
            "cosq": cq, "sinq": sq, "cosk": ck, "sink": sk,
            "m01": m01,
            "ident": ident,
        })
    return in_maps


def kernel(L0_attn_x, L0_attn_w_qkv, L0_attn_w_o, L0_ffn_w1, L0_ffn_w2,
           L0_ffn_w3):
    in_maps = _stage_inputs(L0_attn_x, L0_attn_w_qkv, L0_attn_w_o,
                            L0_ffn_w1, L0_ffn_w2, L0_ffn_w3)
    nc = build_program()
    res = run_bass_kernel_spmd(nc, in_maps, core_ids=list(range(NC8)))
    out = np.empty((4, SEQ, D), dtype=np.float32)
    for c in range(NC8):
        b, half = c // 2, c % 2
        out[b, half * TOK:(half + 1) * TOK] = res.results[c]["out"]
    return out
